# version 09
# speedup vs baseline: 1.3443x; 1.0159x over previous
import numpy as np
import ml_dtypes
import concourse.bass as bass
import concourse.mybir as mybir
from concourse.bass_utils import run_bass_kernel_spmd
from contextlib import ExitStack

F32 = mybir.dt.float32
BF16 = mybir.dt.bfloat16
I32 = mybir.dt.int32
AF = mybir.ActivationFunctionType
ALU = mybir.AluOpType
AX = mybir.AxisListType.X

D = 1024
H = 16
DH = 64
NC_ = 8
T = 2048
NT = 16
TB = 2176
EPS = 1e-5

ENGINES = ("tensor", "vector", "scalar", "gpsimd", "sync")
AUTOCHAIN = {"vector": "ve", "scalar": "ac", "gpsimd": "gp"}


class Em:

    def __init__(self):
        self.q = {k: [] for k in ENGINES}
        self.cnt = {}
        self.mark = {}
        self.semnames = set()
        self.S = None

    def op(self, eng, fn, wait=(), inc=None, dma=False):
        auto = None if dma else AUTOCHAIN.get(eng)
        wait = [w for w in wait if w is not None]
        if hasattr(self, "pending") and eng in self.pending:
            wait = self.pending.pop(eng) + wait
        assert not (dma and inc is None), "dma ops must go through Em.dma"
        if auto is not None:
            wait.append((auto, self.cnt.get(auto, 0)))
            if inc is None:
                inc = (auto, 1)
            else:
                assert inc[0] == auto
        wts = []
        for s, v in wait:
            self.semnames.add(s)
            if v > 0 and self.mark.get((eng, s), 0) < v:
                self.mark[(eng, s)] = v
                wts.append((s, v))
        if inc is not None:
            s, step = inc
            self.semnames.add(s)
            self.cnt[s] = self.cnt.get(s, 0) + step

        def closure(engine, wts=tuple(wts), fn=fn, inc=inc):
            for s, v in wts:
                engine.wait_ge(self.S[s], v)
            ins = fn(engine)
            if inc is not None:
                ins.then_inc(self.S[inc[0]], inc[1])
        self.q[eng].append(closure)
        return (inc[0], self.cnt[inc[0]]) if inc else None

    def stage_barrier(self):
        toks = self.barrier_tokens()
        self.pending = {eng: list(toks) for eng in ENGINES}

    def barrier_tokens(self):
        return [(s, self.cnt[s]) for s in sorted(self.semnames) if self.cnt.get(s, 0) > 0]

    def dma(self, eng, fn, pool="dg", wait=(), size=4):
        if not hasattr(self, "pidx"):
            self.pidx = {}
        idx = self.pidx.get(pool, 0)
        self.pidx[pool] = idx + 1
        sem = "%s%d" % (pool, idx % size)
        prev = self.cnt.get(sem, 0)
        w = [x for x in wait if x is not None]
        if prev:
            w.append((sem, prev))
        return self.op(eng, fn, wait=w, inc=(sem, 16), dma=True)

    def pool_tokens(self, pool, size=4):
        return [(s, self.cnt[s]) for s in ["%s%d" % (pool, i) for i in range(size)]
                if s in self.cnt]

    def raw(self, eng, fn):
        self.q[eng].append(fn)

    def tok(self, s):
        return (s, self.cnt.get(s, 0))

    def play(self, nc, stack):
        self.S = {}
        for s in sorted(self.semnames):
            self.S[s] = stack.enter_context(nc.semaphore(s))
        block = stack.enter_context(nc.Block())
        for eng in ENGINES:
            ops = self.q[eng]
            if not ops:
                continue
            getattr(block, eng)(lambda engine, ops=ops: [f(engine) for f in ops])


def ln_plain(E, xr_ap, out_ap, stats, scr, epsc, wait=()):
    s1 = stats[:, 0:1]
    ss = stats[:, 1:2]
    negmu = stats[:, 2:3]
    musq = stats[:, 3:4]
    vr = stats[:, 4:5]
    var = stats[:, 5:6]
    rstd = stats[:, 6:7]

    def f1(v):
        v.reduce_sum(s1, xr_ap, axis=AX, negate=True)
        return v.scalar_tensor_tensor(scr, xr_ap, 1.0, xr_ap, ALU.mult, ALU.mult,
                                      accum_out=ss)
    t1 = E.op("vector", f1, wait=wait)

    def f2(s):
        s.activation(negmu, s1, AF.Copy, scale=1.0 / D)
        s.activation(musq, s1, AF.Square, scale=1.0 / D)
        return s.activation(vr, ss, AF.Copy, scale=1.0 / D)
    t2 = E.op("scalar", f2, wait=[t1])

    t3 = E.op("vector", lambda v: v.tensor_sub(var, vr, musq), wait=[t2])
    t4 = E.op("scalar", lambda s: s.activation(rstd, var, AF.Ln, bias=epsc), wait=[t3])
    t5 = E.op("scalar", lambda s: s.activation(rstd, rstd, AF.Exp, scale=-0.5), wait=[t4])
    t6 = E.op("vector",
              lambda v: v.tensor_scalar(out_ap, xr_ap, negmu, rstd, ALU.add, ALU.mult),
              wait=[t5])
    return t6



def emit_l0_attn(nc, E, IO):
    st = ExitStack()
    sb = lambda name, shape, dt: st.enter_context(nc.sbuf_tensor(name, shape, dt))
    ps = lambda name, shape, dt=F32: st.enter_context(nc.psum_tensor(name, shape, dt))

    XT = sb("a_XT", [128, 8 * TB], BF16)
    WT = sb("a_WT", [128, 4 * 8 * 1024], BF16)
    BIAS = sb("a_BIAS", [1, 4 * 1024], F32)
    ONE1 = sb("a_ONE1", [1, 128], F32)
    ONESB = sb("a_ONESB", [128, 128], BF16)
    IDB = sb("a_IDB", [128, 128], BF16)
    COS = sb("a_COS", [128, 2 * 1024], F32)
    SIN = sb("a_SIN", [128, 2 * 1024], F32)
    MSKN = sb("a_MSKN", [128, 256], BF16)
    MSKF = sb("a_MSKF", [128, 256], BF16)
    EPSC = sb("a_EPSC", [128, 1], F32)
    STATS = sb("a_STATS", [128, 8], F32)
    SCR = sb("a_SCR", [128, 1024], F32)
    QR = sb("a_QR", [128, 1024], BF16)
    KR = sb("a_KR", [128, 1024], BF16)
    QT = sb("a_QT", [128, 1024], BF16)
    KTR = sb("a_KTR", [128, 3 * 1024], BF16)
    VRG = sb("a_VRG", [128, 3 * 1024], BF16)
    PT = sb("a_PT", [128, 16 * 256], BF16)
    ATT = sb("a_ATT", [128, 8 * 128], BF16)
    SUMS = sb("a_SUMS", [128, 256], F32)
    SUM2 = sb("a_SUM2", [128, 512], F32)
    RBI = sb("a_RBI", [128, 256], F32)
    XIN = sb("a_XIN", [128, 2 * 1024], F32)
    XO = sb("a_XO", [128, 2 * 1024], F32)

    PQ = ps("a_PQ", [128, 1024])
    PK = ps("a_PK", [128, 1024])
    PSB = ps("a_PSB", [128, 1024], BF16)
    PO = ps("a_PO", [128, 1024])
    PSS = ps("a_PSS", [128, 512])

    ltoks = []
    for i, w in enumerate(("WqT", "WkT", "WvT", "WoT")):
        ltoks.append(E.dma("gpsimd",
             lambda g, w=w, i=i: g.dma_start(
                 out=WT[:, i * 8192:(i + 1) * 8192].rearrange("p (c n) -> p c n", c=8),
                 in_=IO[w].rearrange("(c p) n -> p c n", p=128)), pool="di"))
    ltoks.append(E.dma("gpsimd", lambda g: g.dma_start(out=BIAS[:], in_=IO["bqkvo"]), pool="di"))
    ltoks.append(E.dma("gpsimd", lambda g: g.dma_start(out=IDB[:], in_=IO["ident"]), pool="di"))
    ltoks.append(E.dma("gpsimd", lambda g: g.dma_start(out=MSKN[:], in_=IO["maskN"]), pool="di"))
    ltoks.append(E.dma("gpsimd", lambda g: g.dma_start(out=MSKF[:], in_=IO["maskF"]), pool="di"))
    for c in range(8):
        ltoks.append(E.dma("gpsimd",
             lambda g, c=c: g.dma_start(out=XT[:, c * TB:(c + 1) * TB],
                                        in_=IO["x_tr"][c * 128:(c + 1) * 128, :]), pool="di"))
    tload = ltoks

    def finit(v):
        v.memset(ONE1[:], 1.0)
        v.memset(ONESB[:], 1.0)
        return v.memset(EPSC[:], EPS)
    tinit = E.op("vector", finit)

    tqscale = E.op("scalar", lambda s: s.mul(WT[:, 0:8192], WT[:, 0:8192], 0.125),
                   wait=tload)
    tbscale = E.op("scalar", lambda s: s.mul(BIAS[0:1, 0:1024], BIAS[0:1, 0:1024], 0.125))

    state = {
        "tres": None,
        "tVe": None,
        "tQe": None,
        "slot_exp": {},
        "area_evac": {},
        "last_sadd": None,
        "ring_free": {},
        "res_done": {},
        "tbl_free": {},
        "tTk": None, "tTq": None,
    }

    out_toks = []

    def dma(engq, sem, fn, wait=()):
        return E.op(engq, fn, wait=wait, inc=(sem, 16)) if engq == "sync" else \
            E.op(engq, fn, wait=wait)

    NKT = NT + 1
    for kt in range(NKT):
        qt = kt - 1
        slot2 = kt % 2
        slot3 = kt % 3

        wtab = [state["tbl_free"][kt - 2]] if kt - 2 in state["tbl_free"] else []
        ttab1 = E.dma("sync", lambda s, kt=kt, slot2=slot2: s.dma_start(
            out=COS[:, slot2 * 1024:(slot2 + 1) * 1024],
            in_=IO["cosL0"][kt * 128:(kt + 1) * 128, :]), wait=wtab, pool="dt", size=2)
        ttab2 = E.dma("sync", lambda s, kt=kt, slot2=slot2: s.dma_start(
            out=SIN[:, slot2 * 1024:(slot2 + 1) * 1024],
            in_=IO["sinL0"][kt * 128:(kt + 1) * 128, :]), wait=wtab, pool="dt", size=2)
        ttab = [ttab1, ttab2]

        if qt >= 0:
            wxin = [state["res_done"][qt - 2]] if qt - 2 in state["res_done"] else []
            xin_tok = E.dma("sync", lambda s, qt=qt: s.dma_start(
                out=XIN[:, (qt % 2) * 1024:(qt % 2 + 1) * 1024],
                in_=IO["x_tm"][(qt + 1) * 128:(qt + 2) * 128, :]),
                wait=wxin, pool="dx", size=2)

        def proj(which, psum, extra=()):
            wi = {"q": 0, "k": 1, "v": 2}[which]

            def f(t, wi=wi, psum=psum, kt=kt):
                for p_ in range(2):
                    t.matmul(psum[:, p_ * 512:(p_ + 1) * 512], ONE1[:],
                             BIAS[0:1, wi * 1024 + p_ * 512: wi * 1024 + (p_ + 1) * 512],
                             start=True, stop=False)
                ins = None
                for kc in range(8):
                    l = XT[:, kc * TB + kt * 128: kc * TB + (kt + 1) * 128]
                    for p_ in range(2):
                        r = WT[:, wi * 8192 + kc * 1024 + p_ * 512:
                               wi * 8192 + kc * 1024 + (p_ + 1) * 512]
                        ins = t.matmul(psum[:, p_ * 512:(p_ + 1) * 512], l, r,
                                       start=False, stop=(kc == 7))
                return ins
            w = list(tload) + [tinit]
            if which == "q":
                w += [tqscale, tbscale]
            return E.op("tensor", f, wait=list(w) + list(extra), inc=("pe", 1))

        def rope(psum, outb, tproj, buf_free):
            slot = kt % 2
            cs = COS[:, slot * 1024:(slot + 1) * 1024]
            sn = SIN[:, slot * 1024:(slot + 1) * 1024]
            rotv = psum.rearrange("p (h s j) -> p h s j", h=16, s=2, j=32)[:, :, ::-1, :]
            t1 = E.op("vector", lambda v: v.tensor_mul(SCR[:], psum, cs),
                      wait=[tproj] + ttab)
            t2 = E.op("vector", lambda v: v.tensor_mul(
                outb.rearrange("p (h s j) -> p h s j", h=16, s=2, j=32), rotv,
                sn.rearrange("p (h s j) -> p h s j", h=16, s=2, j=32)),
                wait=list(buf_free))
            t3 = E.op("vector", lambda v: v.tensor_add(outb[:], outb[:], SCR[:]))
            return t3

        wk = [state["tVe"]] if state["tVe"] else []
        tK = proj("k", PK, extra=wk)
        tKr = rope(PK[:], KR, tK, [state["tTk"]] if state["tTk"] else [])
        if kt >= 1:
            wq = [state["tres"]] if state["tres"] else []
            tQ = proj("q", PQ, extra=wq)
            tQr = rope(PQ[:], QR, tQ, [state["tTq"]] if state["tTq"] else [])
        tV = proj("v", PK, extra=[tKr])
        wring = [state["ring_free"][slot3]] if slot3 in state["ring_free"] else []
        tVe = E.op("vector", lambda v, slot3=slot3: v.tensor_copy(
            VRG[:, slot3 * 1024:(slot3 + 1) * 1024], PK[:]),
            wait=[tV] + wring)
        state["tVe"] = tVe
        wpsb = [state["tQe"]] if state["tQe"] else []

        def ftk(t):
            ins = None
            for c in range(8):
                ins = t.transpose(PSB[:, c * 128:(c + 1) * 128],
                                  KR[:, c * 128:(c + 1) * 128], IDB[:])
            return ins
        tTk = E.op("tensor", ftk, wait=[tKr] + wpsb, inc=("pe", 1))
        state["tTk"] = tTk
        tKe = E.op("vector", lambda v, slot3=slot3: v.tensor_copy(
            KTR[:, slot3 * 1024:(slot3 + 1) * 1024], PSB[:]),
            wait=[tTk] + wring)
        if kt >= 1:
            def ftq(t):
                ins = None
                for c in range(8):
                    ins = t.transpose(PSB[:, c * 128:(c + 1) * 128],
                                      QR[:, c * 128:(c + 1) * 128], IDB[:])
                return ins
            tTq = E.op("tensor", ftq, wait=[tQr, tKe], inc=("pe", 1))
            state["tTq"] = tTq
            tQe = E.op("vector", lambda v: v.tensor_copy(QT[:], PSB[:]), wait=[tTq])
            state["tQe"] = tQe
        state["tbl_free"][kt] = E.tok("ve")

        if qt < 0:
            continue

        sp = qt % 3
        sd = (qt + 1) % 3
        msk = MSKF if qt == 0 else MSKN
        pair_evac = []
        ring_read_tok = None
        for p in range(8):
            harea = (p % 2) * 512 + ((p // 2) % 2) * 256
            exp_toks = []
            for hi in range(2):
                h = 2 * p + hi
                hb = 64 * (h & 1)
                fc = h >> 1
                scol = 0 if h % 2 == 0 else 512

                def fsc(t, h=h, hb=hb, fc=fc, scol=scol, sp=sp, sd=sd):
                    l1 = KTR[hb:hb + 64, sp * 1024 + fc * 128: sp * 1024 + (fc + 1) * 128]
                    l2 = KTR[hb:hb + 64, sd * 1024 + fc * 128: sd * 1024 + (fc + 1) * 128]
                    r = QT[hb:hb + 64, fc * 128:(fc + 1) * 128]
                    t.matmul(PQ[:, scol:scol + 128], l1, r, start=True, stop=True)
                    return t.matmul(PQ[:, scol + 128:scol + 256], l2, r,
                                    start=True, stop=True)
                wsc = [tQe, tKe, tVe]
                prev = state["slot_exp"].get(h % 2)
                if prev:
                    wsc.append(prev)
                tsc = E.op("tensor", fsc, wait=wsc, inc=("pe", 1))
                texp = E.op("scalar", lambda s, h=h, scol=scol: s.activation(
                    PT[:, h * 256:(h + 1) * 256], PQ[:, scol:scol + 256], AF.Exp),
                    wait=[tsc])
                state["slot_exp"][h % 2] = texp
                tmsk = E.op("vector", lambda v, h=h, msk=msk: v.tensor_mul(
                    PT[:, h * 256:(h + 1) * 256], PT[:, h * 256:(h + 1) * 256], msk[:]),
                    wait=[texp])
                exp_toks.append(tmsk)

            warea = []
            prev_area = state["area_evac"].get(p % 4)
            if prev_area:
                warea.append(prev_area)
            wsadd = [state["last_sadd"]] if state["last_sadd"] else []

            def fpv(t, p=p, harea=harea, sp=sp, sd=sd):
                ins = None
                for hi in range(2):
                    h = 2 * p + hi
                    hb = 64 * (h & 1)
                    o = PO[hb:hb + 64, harea:harea + 128]
                    l1 = VRG[:, sp * 1024 + h * 64: sp * 1024 + (h + 1) * 64]
                    l2 = VRG[:, sd * 1024 + h * 64: sd * 1024 + (h + 1) * 64]
                    t.matmul(o, l1, PT[:, h * 256:h * 256 + 128], start=True, stop=False)
                    ins = t.matmul(o, l2, PT[:, h * 256 + 128:h * 256 + 256],
                                   start=False, stop=True)
                return ins
            tpv = E.op("tensor", fpv, wait=exp_toks + warea + [tVe], inc=("pe", 1))
            ring_read_tok = tpv

            def fsm(t, p=p):
                return t.matmul(PSS[:], ONESB[:], PT[:, p * 512:(p + 1) * 512],
                                start=True, stop=True)
            tsm = E.op("tensor", fsm, wait=exp_toks + wsadd, inc=("pe", 1))

            tscp = E.op("vector", lambda v: v.tensor_copy(SUM2[:], PSS[:]), wait=[tsm])
            def fsadd(v):
                i0 = SUM2[:].rearrange("q (a b j) -> q a b j", a=2, b=2)
                return v.tensor_add(SUMS[:].rearrange("q (a j) -> q a j", a=2),
                                    i0[:, :, 0, :], i0[:, :, 1, :])
            tsadd = E.op("vector", fsadd)
            tsadd_pss_free = tscp
            state["last_sadd"] = tsadd_pss_free
            tln = E.op("scalar", lambda s: s.activation(RBI[:], SUMS[:], AF.Ln),
                       wait=[tsadd])
            tex = E.op("scalar", lambda s: s.activation(RBI[:], RBI[:], AF.Exp,
                                                        scale=-1.0), wait=[tln])
            tev = None
            for hi in range(2):
                h = 2 * p + hi
                hb = 64 * (h & 1)
                fc = h >> 1
                tev = E.op("vector", lambda v, hb=hb, fc=fc, hi=hi, harea=harea:
                           v.tensor_mul(ATT[hb:hb + 64, fc * 128:(fc + 1) * 128],
                                        PO[hb:hb + 64, harea:harea + 128],
                                        RBI[hb:hb + 64, hi * 128:(hi + 1) * 128]),
                           wait=[tex, tpv])
            state["area_evac"][p % 4] = tev
            pair_evac.append(tev)

        state["ring_free"][sp] = ring_read_tok
        last_evac = pair_evac[-1]
        last_exp = state["slot_exp"][1]

        def fo(t):
            for p_ in range(2):
                t.matmul(PQ[:, p_ * 512:(p_ + 1) * 512], ONE1[:],
                         BIAS[0:1, 3 * 1024 + p_ * 512: 3 * 1024 + (p_ + 1) * 512],
                         start=True, stop=False)
            ins = None
            for c in range(8):
                l = ATT[:, c * 128:(c + 1) * 128]
                for p_ in range(2):
                    r = WT[:, 3 * 8192 + c * 1024 + p_ * 512:
                           3 * 8192 + c * 1024 + (p_ + 1) * 512]
                    ins = t.matmul(PQ[:, p_ * 512:(p_ + 1) * 512], l, r,
                                   start=False, stop=(c == 7))
            return ins
        toproj = E.op("tensor", fo, wait=[last_evac, last_exp], inc=("pe", 1))

        xr = XO[:, (qt % 2) * 1024:(qt % 2 + 1) * 1024]
        wres = [toproj, xin_tok]
        if qt >= 2:
            wres.append(out_toks[qt - 2])
        tres = E.op("vector", lambda v, qt=qt, xr=xr: v.tensor_add(
            xr, XIN[:, (qt % 2) * 1024:(qt % 2 + 1) * 1024], PQ[:]),
            wait=wres)
        state["tres"] = tres
        state["res_done"][qt] = tres
        tln2 = ln_plain(E, xr, xr, STATS[:], SCR[:], EPSC[:], wait=[tres])
        out_toks.append(E.dma("sync", lambda s, qt=qt, xr=xr: s.dma_start(
            out=IO["out_x1"][qt * 128:(qt + 1) * 128, :], in_=xr),
            wait=[tln2], pool="do", size=2))

    E.op("sync", lambda s: s.nop(), wait=E.pool_tokens("do", 2), inc=None)
    return st



def l0_rope_tables(positions):
    half = 32
    inv = (1.0 / 10000.0 ** (np.arange(half, dtype=np.float32) / half)).astype(np.float32)
    ang = positions.astype(np.float32)[:, None] * inv[None, :]
    c = np.cos(ang).astype(np.float32)
    s = np.sin(ang).astype(np.float32)
    cos64 = np.concatenate([c, c], axis=1)
    sin64 = np.concatenate([-s, s], axis=1)
    return (np.ascontiguousarray(np.tile(cos64, (1, 16))),
            np.ascontiguousarray(np.tile(sin64, (1, 16))))


def l0_masks():
    p = np.arange(128)[:, None]
    j = np.arange(128)[None, :]
    mD = (p <= j).astype(ml_dtypes.bfloat16)
    mN = (p >= j).astype(ml_dtypes.bfloat16)
    return mN, mD


def prep_stage_a(I):
    x = np.asarray(I["L0_attn_x"], dtype=np.float32)
    WqT = np.ascontiguousarray(np.asarray(I["L0_attn_Wq"]).T)
    WkT = np.ascontiguousarray(np.asarray(I["L0_attn_Wk"]).T)
    WvT = np.ascontiguousarray(np.asarray(I["L0_attn_Wv"]).T)
    WoT = np.ascontiguousarray(np.asarray(I["L0_attn_Wo"]).T)
    bias = np.ascontiguousarray(np.stack([I["L0_attn_bq"], I["L0_attn_bk"],
                                          I["L0_attn_bv"], I["L0_attn_bo"]])
                                .reshape(1, 4 * D).astype(np.float32))
    mN, mD = l0_masks()
    maskN = np.ascontiguousarray(np.concatenate([mN, mD], axis=1))
    maskF0 = np.ascontiguousarray(np.concatenate([np.zeros_like(mN), mD], axis=1))
    ident = np.eye(128, dtype=np.float32)
    maps = []
    for c in range(NC_):
        b, h = c // 2, c % 2
        start = h * T
        if h == 0:
            xs = np.concatenate([np.zeros((128, D), np.float32), x[b, 0:T]], axis=0)
            pos = np.concatenate([np.zeros(128), np.arange(T)]).astype(np.float32)
            maskF = maskF0
        else:
            xs = x[b, start - 128:start + T]
            pos = np.arange(start - 128, start + T).astype(np.float32)
            maskF = maskN
        cos, sin = l0_rope_tables(pos)
        maps.append({
            "x_tm": np.ascontiguousarray(xs),
            "x_tr": np.ascontiguousarray(xs.T),
            "WqT": WqT, "WkT": WkT, "WvT": WvT, "WoT": WoT,
            "bqkvo": bias, "cosL0": cos, "sinL0": sin,
            "maskN": maskN, "maskF": maskF,
            "ident": ident,
        })
    return maps


def build_stage_a(nc):
    IO = {}
    IO["x_tm"] = nc.dram_tensor("x_tm", [TB, D], F32, kind="ExternalInput").ap()
    IO["x_tr"] = nc.dram_tensor("x_tr", [D, TB], F32, kind="ExternalInput").ap()
    for w in ("WqT", "WkT", "WvT", "WoT"):
        IO[w] = nc.dram_tensor(w, [D, D], F32, kind="ExternalInput").ap()
    IO["bqkvo"] = nc.dram_tensor("bqkvo", [1, 4 * D], F32, kind="ExternalInput").ap()
    IO["cosL0"] = nc.dram_tensor("cosL0", [TB, D], F32, kind="ExternalInput").ap()
    IO["sinL0"] = nc.dram_tensor("sinL0", [TB, D], F32, kind="ExternalInput").ap()
    IO["maskN"] = nc.dram_tensor("maskN", [128, 256], BF16, kind="ExternalInput").ap()
    IO["maskF"] = nc.dram_tensor("maskF", [128, 256], BF16, kind="ExternalInput").ap()
    IO["ident"] = nc.dram_tensor("ident", [128, 128], F32, kind="ExternalInput").ap()
    IO["out_x1"] = nc.dram_tensor("out_x1", [T, D], F32, kind="ExternalOutput").ap()
    E = Em()
    stc = emit_l0_attn(nc, E, IO)
    with ExitStack() as stack:
        E.play(nc, stack)
    stc.close()
    return nc


def run_stage_a(I):
    nc = bass.Bass("TRN2", target_bir_lowering=False)
    build_stage_a(nc)
    maps = prep_stage_a(I)
    res = run_bass_kernel_spmd(nc, maps, core_ids=list(range(NC_)))
    x1 = np.empty((4, 4096, 1024), np.float32)
    for c in range(NC_):
        b, h = c // 2, c % 2
        x1[b, h * T:(h + 1) * T] = res.results[c]["out_x1"]
    return x1



B_PARTS = 3

def kern_B_PARTS():
    return 3

def kern_B_TILES():
    return 16

def emit_l0_ffn(nc, E, IO, fused=False, w0=()):
    st = ExitStack()
    sb = lambda name, shape, dt: st.enter_context(nc.sbuf_tensor(name, shape, dt))
    ps = lambda name, shape, dt=F32: st.enter_context(nc.psum_tensor(name, shape, dt))

    X = sb("b_X", [128, 16 * 1024], F32)
    NTB = sb("b_NTB", [128, 8 * 2048], BF16)
    NTF = sb("b_NTF", [128, 1024], F32)
    GWB = sb("b_GWB", [128, 8 * 1024], BF16)
    UWB = sb("b_UWB", [128, 8 * 1024], BF16)
    DWB = sb("b_DWB", [128, 8 * 1024], BF16)
    RWF = sb("b_RWF", [128, 8 * 8], F32)
    WREP = sb("b_WREP", [128, 1024], F32)
    BREP = sb("b_BREP", [128, 1024], F32)
    LNWB = sb("b_LNWB", [1, 2048], F32)
    ONE1 = sb("b_ONE1", [1, 128], F32)
    IDF = sb("b_IDF", [128, 128], F32)
    EPSC = sb("b_EPSC", [128, 1], F32)
    STATS = sb("b_STATS", [128, 8], F32)
    SCR = sb("b_SCR", [128, 1024], F32)
    NR = sb("b_NR", [128, 1024], F32)
    LG = sb("b_LG", [128, 16 * 8], F32)
    MX = sb("b_MX", [128, 16], F32)
    SCO = sb("b_SCO", [128, 16], F32)
    W8 = sb("b_W8", [128, 16 * 8], F32)
    W8T = sb("b_W8T", [8, 16 * 128], F32)
    W8R = sb("b_W8R", [1, 512], F32)
    WBC = sb("b_WBC", [128, 512], F32)
    G2 = sb("b_G2", [128, 512], F32)
    S2 = sb("b_S2", [128, 512], BF16)
    U2 = sb("b_U2", [128, 512], BF16)
    HMA = sb("b_HMA", [128, 8 * 512], BF16)
    XO = sb("b_XO", [128, 1024], F32)
    XOB = sb("b_XOB", [128, 1024], BF16) if fused else None
    NSB = sb("b_NSB", [1, 8], I32) if fused else None

    PYA = ps("b_PYA", [128, 1024])
    PYB = ps("b_PYB", [128, 1024])
    PG = ps("b_PG", [128, 512])
    PGB = ps("b_PGB", [128, 512])
    PU = ps("b_PU", [128, 512])
    PSM = ps("b_PSM", [128, 512])

    ltoks = []
    ltoks.append(E.dma("gpsimd", lambda g: g.dma_start(
        out=X[:].rearrange("p (t d) -> p t d", t=16),
        in_=IO["x1"].rearrange("(t p) d -> p t d", p=128)), wait=list(w0), pool="di"))
    ltoks.append(E.dma("gpsimd", lambda g: g.dma_start(out=LNWB[:], in_=IO["lnwb"]), pool="di"))
    ltoks.append(E.dma("gpsimd", lambda g: g.dma_start(
        out=RWF[:].rearrange("p (c n) -> p c n", c=8),
        in_=IO["router_w"].rearrange("(c p) n -> p c n", p=128)), pool="di"))
    ltoks.append(E.dma("gpsimd", lambda g: g.dma_start(out=IDF[:], in_=IO["ident"]), pool="di"))
    tload = ltoks

    def finit(v):
        v.memset(ONE1[:], 1.0)
        return v.memset(EPSC[:], EPS)
    tinit = E.op("vector", finit)

    def fbc(t):
        for p_ in range(2):
            t.matmul(PYA[:, p_ * 512:(p_ + 1) * 512], ONE1[:],
                     LNWB[0:1, p_ * 512:(p_ + 1) * 512], start=True, stop=True)
            t.matmul(PYB[:, p_ * 512:(p_ + 1) * 512], ONE1[:],
                     LNWB[0:1, 1024 + p_ * 512: 1024 + (p_ + 1) * 512],
                     start=True, stop=True)
        return t.matmul(PYB[:, 512:1024], ONE1[:], LNWB[0:1, 1536:2048],
                        start=True, stop=True)
    tbc = E.op("tensor", fbc, wait=list(tload) + [tinit], inc=("pe", 1))
    twr = E.op("vector", lambda v: v.tensor_copy(WREP[:], PYA[:]), wait=[tbc])
    tbr = E.op("vector", lambda v: v.tensor_copy(BREP[:], PYB[:]), wait=[tbc])

    tr_evac = tbr
    router_done = []
    ttp_prev = None
    for t_ in range(kern_B_TILES()):
        xt = X[:, t_ * 1024:(t_ + 1) * 1024]
        wln = list(tload)
        if ttp_prev:
            wln.append(ttp_prev)
        tln = ln_plain(E, xt, NR[:], STATS[:], SCR[:], EPSC[:], wait=wln)
        t_w = E.op("vector", lambda v: v.tensor_mul(NR[:], NR[:], WREP[:]), wait=[twr])
        t_b = E.op("vector", lambda v: v.tensor_add(NR[:], NR[:], BREP[:]), wait=[tbr])

        def ftp(t):
            ins = None
            for c in range(8):
                ins = t.transpose(PYA[:, c * 128:(c + 1) * 128],
                                  NR[:, c * 128:(c + 1) * 128], IDF[:])
            return ins
        ttp = E.op("tensor", ftp, wait=[t_b, tr_evac], inc=("pe", 1))
        ttp_prev = ttp
        tef = E.op("vector", lambda v: v.tensor_copy(NTF[:], PYA[:]), wait=[ttp])
        teb = E.op("scalar", lambda s, t_=t_: s.activation(
            NTB[:].rearrange("p (c n) -> p c n", c=8)[:, :, t_ * 128:(t_ + 1) * 128],
            PYA[:].rearrange("p (c n) -> p c n", c=8), AF.Copy), wait=[ttp, tef])
        tr_evac = teb

        def frt(t):
            ins = None
            for kc in range(8):
                ins = t.matmul(PSM[:, 0:8], NTF[:, kc * 128:(kc + 1) * 128],
                               RWF[:, kc * 8:(kc + 1) * 8],
                               start=(kc == 0), stop=(kc == 7))
            return ins
        trt = E.op("tensor", frt, wait=[tef, teb], inc=("pe", 1))
        tlg = E.op("vector", lambda v, t_=t_: v.tensor_copy(
            LG[:, t_ * 8:(t_ + 1) * 8], PSM[:, 0:8]), wait=[trt])
        tmx = E.op("vector", lambda v, t_=t_: v.reduce_max(
            MX[:, t_:t_ + 1], LG[:, t_ * 8:(t_ + 1) * 8], axis=AX))
        router_done.append(tmx)

    if kern_B_PARTS() < 1:
        for t_ in range(16):
            E.dma("sync", lambda s, t_=t_: s.dma_start(
                out=IO["out_x2"][t_ * 128:(t_ + 1) * 128, :],
                in_=X[:, t_ * 1024:(t_ + 1) * 1024]),
                wait=[router_done[-1], tr_evac], pool="do", size=2)
        E.op("sync", lambda s: s.nop(), wait=E.pool_tokens("do", 2), inc=None)
        return st
    tsg = E.op("scalar", lambda s: s.activation(SCO[:], MX[:], AF.Sigmoid),
               wait=[router_done[-1]])
    w8_done = []
    for t_ in range(16):
        tm = E.op("vector", lambda v, t_=t_: v.tensor_scalar(
            W8[:, t_ * 8:(t_ + 1) * 8], LG[:, t_ * 8:(t_ + 1) * 8],
            MX[:, t_:t_ + 1], SCO[:, t_:t_ + 1], ALU.is_ge, ALU.mult), wait=[tsg])
        w8_done.append(tm)
        tx2 = E.op("vector", lambda v, t_=t_: v.tensor_scalar_mul(
            X[:, t_ * 1024:(t_ + 1) * 1024], X[:, t_ * 1024:(t_ + 1) * 1024], 2.0))
    tw8t_prev = None
    for t_ in range(16):
        ftw = lambda t, t_=t_: t.transpose(PSM[0:8, 0:128], W8[:, t_ * 8:(t_ + 1) * 8],
                                           IDF[:])
        w = [w8_done[t_], E.tok("ve")]
        if tw8t_prev:
            w.append(tw8t_prev)
        ttw = E.op("tensor", ftw, wait=w, inc=("pe", 1))
        tw8t_prev = E.op("vector", lambda v, t_=t_: v.tensor_copy(
            W8T[:, t_ * 128:(t_ + 1) * 128], PSM[0:8, 0:128]), wait=[ttw])
    tW8T = E.dma("gpsimd", lambda g: g.dma_start(out=IO["w8scr"], in_=W8T[:]),
                 wait=[tw8t_prev], pool="dr", size=2)

    if kern_B_PARTS() < 2:
        for t_ in range(16):
            E.dma("sync", lambda s, t_=t_: s.dma_start(
                out=IO["out_x2"][t_ * 128:(t_ + 1) * 128, :],
                in_=X[:, t_ * 1024:(t_ + 1) * 1024]),
                wait=[tW8T, E.tok("ve")], pool="do", size=2)
        E.op("sync", lambda s: s.nop(), wait=E.pool_tokens("do", 2), inc=None)
        return st
    state = {"wbc_evac": None, "xadd": {}, "t2": None, "hm_used": None}
    tmemset_wbc = None
    steps = [(e, fh) for e in range(9) for fh in range(2)]
    for i, (e, fh) in enumerate(steps):
        sl = 0
        wdep = [state["xadd"].get(i - 1)]
        if e < 8:
            tg = E.dma("gpsimd", lambda g, e=e, fh=fh, sl=sl: g.dma_start(
                out=GWB[:, sl * 8192:(sl + 1) * 8192].rearrange("p (c n) -> p c n", c=8),
                in_=IO["gate_w"][e].rearrange("(c p) n -> p c n", p=128)
                [:, :, fh * 1024:(fh + 1) * 1024]), wait=wdep, pool="dw", size=6)
            tu = E.dma("gpsimd", lambda g, e=e, fh=fh, sl=sl: g.dma_start(
                out=UWB[:, sl * 8192:(sl + 1) * 8192].rearrange("p (c n) -> p c n", c=8),
                in_=IO["up_w"][e].rearrange("(c p) n -> p c n", p=128)
                [:, :, fh * 1024:(fh + 1) * 1024]), wait=wdep, pool="dw", size=6)
            td = E.dma("gpsimd", lambda g, e=e, fh=fh, sl=sl: g.dma_start(
                out=DWB[:, sl * 8192:(sl + 1) * 8192].rearrange("p (c n) -> p c n", c=8),
                in_=IO["down_w"][e].rearrange("(f p) n -> p f n", p=128)
                [:, fh * 8:(fh + 1) * 8, :]), wait=wdep, pool="dw", size=6)
        else:
            tg = E.dma("gpsimd", lambda g, fh=fh, sl=sl: g.dma_start(
                out=GWB[:, sl * 8192:(sl + 1) * 8192].rearrange("p (c n) -> p c n", c=8),
                in_=IO["sh_gate"].rearrange("(c p) n -> p c n", p=128)
                [:, :, fh * 1024:(fh + 1) * 1024]), wait=wdep, pool="dw", size=6)
            tu = E.dma("gpsimd", lambda g, fh=fh, sl=sl: g.dma_start(
                out=UWB[:, sl * 8192:(sl + 1) * 8192].rearrange("p (c n) -> p c n", c=8),
                in_=IO["sh_up"].rearrange("(c p) n -> p c n", p=128)
                [:, :, fh * 1024:(fh + 1) * 1024]), wait=wdep, pool="dw", size=6)
            td = E.dma("gpsimd", lambda g, fh=fh, sl=sl: g.dma_start(
                out=DWB[:, sl * 8192:(sl + 1) * 8192].rearrange("p (c n) -> p c n", c=8),
                in_=IO["sh_down"].rearrange("(f p) n -> p f n", p=128)
                [:, fh * 8:(fh + 1) * 8, :]), wait=wdep, pool="dw", size=6)
        twload = [tg, tu, td]
        if e == 8 and fh == 0:
            tmemset_wbc = E.op("vector", lambda v: v.memset(WBC[:], 1.0),
                               wait=[state["wbc_evac"]])
        last_add_b = state["xadd"].get(i - 1) or state["xadd"].get(i - 2)
        for blk in range(4):
            if e < 8:
                wpw = [state["wbc_evac"]] if state["wbc_evac"] else []
                tw8r = E.dma("gpsimd", lambda g, e=e, blk=blk: g.dma_start(
                    out=W8R[:], in_=IO["w8scr"][e:e + 1, blk * 512:(blk + 1) * 512]),
                    wait=[tW8T] + wpw, pool="dr", size=2)
                fpw = lambda t, e=e, blk=blk: t.matmul(
                    PSM[:], ONE1[:], W8R[:], start=True, stop=True)
                tpw = E.op("tensor", fpw, wait=list(twload) + wpw + [tw8r], inc=("pe", 1))
                twbc = E.op("vector", lambda v: v.tensor_copy(WBC[:], PSM[:]),
                            wait=[tpw])
                state["wbc_evac"] = twbc
            else:
                twbc = tmemset_wbc
            xadd_prev = last_add_b
            hm_prev = state["hm_used"]
            for fc in range(8):
                PGc = PG if fc % 2 == 0 else PGB
                def fg(t, blk=blk, fc=fc, sl=sl, PGc=PGc):
                    ins = None
                    for kc in range(8):
                        ins = t.matmul(PGc[:], GWB[:, sl * 8192 + kc * 1024 + fc * 128:
                                                  sl * 8192 + kc * 1024 + (fc + 1) * 128],
                                       NTB[:, kc * 2048 + blk * 512:
                                           kc * 2048 + (blk + 1) * 512],
                                       start=(kc == 0), stop=(kc == 7))
                    return ins
                wg = list(twload) + [tr_evac]
                if state.get("t1p"):
                    wg.append(state["t1p"])
                tgm = E.op("tensor", fg, wait=wg, inc=("pe", 1))

                def fu(t, blk=blk, fc=fc, sl=sl):
                    ins = None
                    for kc in range(8):
                        ins = t.matmul(PU[:], UWB[:, sl * 8192 + kc * 1024 + fc * 128:
                                                  sl * 8192 + kc * 1024 + (fc + 1) * 128],
                                       NTB[:, kc * 2048 + blk * 512:
                                           kc * 2048 + (blk + 1) * 512],
                                       start=(kc == 0), stop=(kc == 7))
                    return ins
                wu = [state["t2"]] if state["t2"] else []
                tum = E.op("tensor", fu, wait=wu, inc=("pe", 1))

                t1 = E.op("vector", lambda v, PGc=PGc: v.tensor_mul(G2[:], PGc[:], WBC[:]),
                          wait=[tgm, twbc])
                t2 = E.op("vector", lambda v: v.tensor_mul(U2[:], PU[:], WBC[:]),
                          wait=[tum])
                state["t1p"] = state.get("t1c")
                state["t1c"] = t1
                state["t2"] = t2
                ts_ = E.op("scalar", lambda s: s.activation(S2[:], G2[:], AF.Silu),
                           wait=[t1])
                wh = [ts_]
                if fc == 0 and hm_prev:
                    wh.append(hm_prev)
                th = E.op("vector", lambda v, fc=fc: v.tensor_mul(
                    HMA[:, fc * 512:(fc + 1) * 512], S2[:], U2[:]), wait=wh)
            tdm = None
            for h2 in range(2):
                def fd(t, blk=blk, h2=h2, sl=sl):
                    ins = None
                    for fc in range(8):
                        for sub in range(2):
                            l = HMA[:, fc * 512 + h2 * 256 + sub * 128:
                                    fc * 512 + h2 * 256 + (sub + 1) * 128]
                            for p_ in range(2):
                                ins = t.matmul(
                                    (PYA if sub == 0 else PYB)[:, p_ * 512:(p_ + 1) * 512],
                                    l,
                                    DWB[:, sl * 8192 + fc * 1024 + p_ * 512:
                                        sl * 8192 + fc * 1024 + (p_ + 1) * 512],
                                    start=(fc == 0), stop=(fc == 7))
                    return ins
                wd = [th]
                if xadd_prev:
                    wd.append(xadd_prev)
                if tdm:
                    wd.append(tadd2)
                tdm = E.op("tensor", fd, wait=wd, inc=("pe", 1))
                tt0 = 2 * blk * 1024 + h2 * 512
                base = (4 * blk + 2 * h2) * 1024
                tadd1 = E.op("vector", lambda v, base=base: v.tensor_add(
                    X[:, base:base + 1024], X[:, base:base + 1024], PYA[:]),
                    wait=[tdm])
                tadd2 = E.op("vector", lambda v, base=base: v.tensor_add(
                    X[:, base + 1024:base + 2048], X[:, base + 1024:base + 2048], PYB[:]))
            state["hm_used"] = tdm
            last_add_b = tadd2
            state["xadd"][i] = tadd2
    state["xadd"] = state["xadd"][len(steps) - 1]

    if kern_B_PARTS() < 3:
        for t_ in range(16):
            E.dma("sync", lambda s, t_=t_: s.dma_start(
                out=IO["out_x2"][t_ * 128:(t_ + 1) * 128, :],
                in_=X[:, t_ * 1024:(t_ + 1) * 1024]),
                wait=[state["xadd"]], pool="do", size=2)
        E.op("sync", lambda s: s.nop(), wait=E.pool_tokens("do", 2), inc=None)
        return st
    otoks = {}
    if fused:
        tnsb = E.dma("gpsimd", lambda g: g.dma_start(out=NSB[:], in_=IO["nonce"]),
                     pool="di")
    for t_ in range(16):
        xt = X[:, t_ * 1024:(t_ + 1) * 1024]
        xo = XO[:]
        wln = [state["xadd"]]
        if t_ - 1 in otoks:
            wln.append(otoks[t_ - 1])
        if (t_ - 1, "b") in otoks:
            wln.append(otoks[(t_ - 1, "b")])
        tln = ln_plain(E, xt, xo, STATS[:], SCR[:], EPSC[:], wait=wln)
        otoks[t_] = E.dma("sync", lambda s, t_=t_, xo=xo: s.dma_start(
            out=IO["out_x2"][t_ * 128:(t_ + 1) * 128, :], in_=xo),
            wait=[tln], pool="do", size=2)
        if fused:
            tcb = E.op("vector", lambda v, xo=xo: v.tensor_copy(XOB[:], xo), wait=[tln])
            otoks[(t_, "b")] = E.dma("sync", lambda s, t_=t_: s.dma_start(
                out=IO["X2bf_d"][t_ * 128:(t_ + 1) * 128, :], in_=XOB[:]),
                wait=[tcb], pool="do", size=2)
    if fused:
        allout = E.pool_tokens("do", 2)
        def fxch(g):
            for arm in g.switch_core_id(n=128):
                h = arm.logical % 2
                g.dma_start(out=IO["SHX"][h * T:(h + 1) * T, :],
                            in_=IO["X2bf_d"][:]).then_inc(E.S["dsh"], 16)
                g.wait_ge(E.S["dsh"], 16)
                g.dma_start(out=IO["FLAG"][h:h + 1, :],
                            in_=NSB[:]).then_inc(E.S["dsh"], 16)
            return g.memset(W8R[0:1, 0:1], 0.0)
        E.op("gpsimd", fxch, wait=allout + [tnsb, ("dsh", 0)], inc=("gp", 1))
    E.op("sync", lambda s: s.nop(), wait=E.pool_tokens("do", 2), inc=None)
    return st


def prep_stage_b(I, x1):
    lnwb = np.concatenate([I["L0_ffn_ln_w"], I["L0_ffn_ln_b"]]).reshape(1, 2048).astype(np.float32)
    ident = np.eye(128, dtype=np.float32)
    common = {
        "lnwb": np.ascontiguousarray(lnwb),
        "router_w": np.ascontiguousarray(I["L0_ffn_router_w"].astype(np.float32)),
        "gate_w": np.ascontiguousarray(I["L0_ffn_gate_w"].astype(np.float32)),
        "up_w": np.ascontiguousarray(I["L0_ffn_up_w"].astype(np.float32)),
        "down_w": np.ascontiguousarray(I["L0_ffn_down_w"].astype(np.float32)),
        "sh_gate": np.ascontiguousarray(I["L0_ffn_sh_gate"].astype(np.float32)),
        "sh_up": np.ascontiguousarray(I["L0_ffn_sh_up"].astype(np.float32)),
        "sh_down": np.ascontiguousarray(I["L0_ffn_sh_down"].astype(np.float32)),
        "ident": ident,
    }
    maps = []
    for c in range(NC_):
        b, h = c // 2, c % 2
        m = dict(common)
        m["x1"] = np.ascontiguousarray(x1[b, h * T:(h + 1) * T])
        maps.append(m)
    return maps


def build_stage_b(nc):
    IO = {}
    IO["x1"] = nc.dram_tensor("x1", [T, D], F32, kind="ExternalInput").ap()
    IO["lnwb"] = nc.dram_tensor("lnwb", [1, 2048], F32, kind="ExternalInput").ap()
    IO["router_w"] = nc.dram_tensor("router_w", [D, 8], F32, kind="ExternalInput").ap()
    IO["gate_w"] = nc.dram_tensor("gate_w", [8, D, 2048], F32, kind="ExternalInput").ap()
    IO["up_w"] = nc.dram_tensor("up_w", [8, D, 2048], F32, kind="ExternalInput").ap()
    IO["down_w"] = nc.dram_tensor("down_w", [8, 2048, D], F32, kind="ExternalInput").ap()
    IO["sh_gate"] = nc.dram_tensor("sh_gate", [D, 2048], F32, kind="ExternalInput").ap()
    IO["sh_up"] = nc.dram_tensor("sh_up", [D, 2048], F32, kind="ExternalInput").ap()
    IO["sh_down"] = nc.dram_tensor("sh_down", [2048, D], F32, kind="ExternalInput").ap()
    IO["ident"] = nc.dram_tensor("ident", [128, 128], F32, kind="ExternalInput").ap()
    IO["out_x2"] = nc.dram_tensor("out_x2", [T, D], F32, kind="ExternalOutput").ap()
    IO["w8scr"] = nc.dram_tensor("w8scr", [8, 2048], F32, kind="Internal").ap()
    E = Em()
    stc = emit_l0_ffn(nc, E, IO)
    with ExitStack() as stack:
        E.play(nc, stack)
    stc.close()
    return nc


def run_stage_b(I, x1):
    nc = bass.Bass("TRN2", target_bir_lowering=False)
    build_stage_b(nc)
    maps = prep_stage_b(I, x1)
    res = run_bass_kernel_spmd(nc, maps, core_ids=list(range(NC_)))
    x2 = np.empty((4, 4096, 1024), np.float32)
    for c in range(NC_):
        b, h = c // 2, c % 2
        x2[b, h * T:(h + 1) * T] = res.results[c]["out_x2"]
    return x2



def emit_l1_attn(nc, E, IO, fused=False, w0=()):
    st = ExitStack()
    sb = lambda name, shape, dt: st.enter_context(nc.sbuf_tensor(name, shape, dt))

    ONE1 = sb("c_ONE1", [1, 128], F32)
    ONEC = sb("c_ONEC", [128, 64], F32)
    IDB = sb("c_IDB", [128, 128], BF16)
    EPSC = sb("c_EPSC", [128, 1], F32)
    STATS = sb("c_STATS", [128, 8], F32)
    SCR = sb("c_SCR", [128, 1024], F32)
    XIN = sb("c_XIN", [128, 1024], F32)

    ltoks = []
    ltoks.append(E.dma("gpsimd", lambda g: g.dma_start(out=IDB[:], in_=IO["c_ident"]), pool="di"))
    tload = ltoks

    def finit(v):
        v.memset(ONE1[:], 1.0)
        v.memset(ONEC[:], 1.0)
        return v.memset(EPSC[:], EPS)
    tinit = E.op("vector", finit)

    txch = None
    if fused:
        NSB2 = st.enter_context(nc.sbuf_tensor("c_NSB", [1, 8], I32))
        FSB = st.enter_context(nc.sbuf_tensor("c_FSB", [1, 8], I32))
        PACE = st.enter_context(nc.sbuf_tensor("c_PACE", [1, 512], F32))
        tn2 = E.dma("gpsimd", lambda g: g.dma_start(out=NSB2[:], in_=IO["nonce"]),
                    pool="di")
        def fpoll(g):
            g.memset(FSB[:], 0)
            rflag = g.alloc_register("rflag")
            rnonce = g.alloc_register("rnonce")
            rdiff = g.alloc_register("rdiff")
            g.reg_load(rnonce, NSB2[0:1, 0:1])
            def cond():
                g.reg_load(rflag, FSB[0:1, 4:5])
                g.reg_alu(rdiff, rflag, rnonce, ALU.bitwise_xor)
                return rdiff
            for arm in g.switch_core_id(n=128):
                h = arm.logical % 2
                with g.While(cond):
                    g.dma_start(out=FSB[0:1, 4:5], in_=IO["FLAG"][1 - h:2 - h, 0:1])                        .then_inc(E.S["pl"], 16)
                    g.memset(PACE[:], 0.0)
            return g.memset(PACE[0:1, 0:8], 0.0)
        txch = E.op("gpsimd", fpoll, wait=[tn2, ("pl", 0)], inc=("gp", 1))

    with ExitStack() as st1:
        sb1 = lambda name, shape, dt: st1.enter_context(nc.sbuf_tensor(name, shape, dt))
        ps1 = lambda name, shape, dt=F32: st1.enter_context(nc.psum_tensor(name, shape, dt))
        WB = sb1("c_WB", [128, 4 * 8 * 1024], BF16)
        BIAS = sb1("c_BIAS", [1, 4 * 1024], F32)
        COS = sb1("c_COS", [128, 2 * 1024], F32)
        SIN = sb1("c_SIN", [128, 2 * 1024], F32)
        RR = sb1("c_RR", [128, 1024], BF16)
        XTK = sb1("c_XTK", [128, 2 * 8 * 512], BF16)
        KTB = sb1("c_KTB", [128, 8 * 512], BF16)
        VSB = sb1("c_VSB", [128, 4 * 1040], BF16)
        PK = ps1("c_PK", [128, 1024])
        PV_ = ps1("c_PV", [128, 1024])
        PSB = ps1("c_PSB", [128, 1024], BF16)
        for i, w in enumerate(("Wq", "Wk", "Wv", "Wo")):
            ltoks.append(E.dma("gpsimd",
                lambda g, w=w, i=i: g.dma_start(
                    out=WB[:, i * 8192:(i + 1) * 8192].rearrange("p (c n) -> p c n", c=8),
                    in_=IO[w].rearrange("(c p) n -> p c n", p=128)), pool="di"))
        ltoks.append(E.dma("gpsimd", lambda g: g.dma_start(out=BIAS[:], in_=IO["bqkvo"]), pool="di"))
        tqsc = E.op("scalar", lambda s: s.mul(WB[:, 0:8192], WB[:, 0:8192], 0.125),
                    wait=list(ltoks))
        tbsc = E.op("scalar", lambda s: s.mul(BIAS[0:1, 0:1024], BIAS[0:1, 0:1024], 0.125))

        tvones = E.op("vector", lambda v: v.memset(
            VSB[:].rearrange("p (t h o) -> p t h o", t=4, h=16)[:, :, :, 64:65], 1.0))

        state = {"xtk": {}, "ktb_out": None, "vsb_out": None, "psb": None,
                 "tT": None, "rope_w": None, "tbl": {}, "tVe": None, "tKe": None,
                 "gl": 0, "xg": 0, "slot_pe": {}}

        def rope_c(psum, tproj, ttabs, buf_free):
            slotc = state["ropeslot"]
            cs = COS[:, slotc * 1024:(slotc + 1) * 1024]
            sn = SIN[:, slotc * 1024:(slotc + 1) * 1024]
            rotv = psum.rearrange("p (j s) -> p j s", s=2)[:, :, ::-1]
            t1 = E.op("vector", lambda v: v.tensor_mul(SCR[:], psum, cs),
                      wait=[tproj] + ttabs)
            t2 = E.op("vector", lambda v: v.tensor_mul(
                RR[:].rearrange("p (j s) -> p j s", s=2), rotv,
                sn.rearrange("p (j s) -> p j s", s=2)), wait=list(buf_free))
            t3 = E.op("vector", lambda v: v.tensor_add(RR[:], RR[:], SCR[:]))
            return t3

        def build_pass(which, nblk, src_xT, src_cos, src_sin, out_dram, pos_base):
            for blk in range(nblk):
                xg = state["xg"]
                state["xg"] = xg + 1
                slot = xg % 2
                wx = [state["xtk"][xg - 2]] if xg - 2 in state["xtk"] else []
                wx += list(w0)
                if slot in state["slot_pe"]:
                    wx.append(state["slot_pe"][slot])
                if fused:
                    txts = []
                    for c8 in range(8):
                        txts.append(E.dma("sync", lambda s, blk=blk, slot=slot, c8=c8:
                                    s.dma_start(
                                        out=XTK[:, slot * 4096 + c8 * 512:
                                                slot * 4096 + (c8 + 1) * 512],
                                        in_=src_xT[blk * 512:(blk + 1) * 512,
                                                   c8 * 128:(c8 + 1) * 128],
                                        transpose=True),
                                    wait=wx + ([txch] if txch else []),
                                    pool="dk", size=2))
                    txt_list = txts[-2:]
                    txt = txts[-1]
                else:
                    txt = E.dma("gpsimd", lambda g, blk=blk, slot=slot: g.dma_start(
                        out=XTK[:, slot * 4096:(slot + 1) * 4096]
                        .rearrange("p (c k) -> p c k", c=8),
                        in_=src_xT[:, blk * 512:(blk + 1) * 512]
                        .rearrange("(c p) k -> p c k", p=128)), wait=wx, pool="dk", size=2)
                    txt_list = [txt]
                state["xtk"][xg] = txt
                for kt in range(4):
                    row = blk * 4 + kt
                    gl = state["gl"]
                    state["gl"] = gl + 1
                    tslot = gl % 2
                    state["ropeslot"] = tslot
                    wtb = [state["tbl"][gl - 2]] if gl - 2 in state["tbl"] else []
                    tt1 = E.dma("sync", lambda s, row=row, tslot=tslot, src_cos=src_cos:
                                s.dma_start(out=COS[:, tslot * 1024:(tslot + 1) * 1024],
                                            in_=src_cos[row * 128:(row + 1) * 128, :]),
                                wait=wtb, pool="dt", size=2)
                    tt2 = E.dma("sync", lambda s, row=row, tslot=tslot, src_sin=src_sin:
                                s.dma_start(out=SIN[:, tslot * 1024:(tslot + 1) * 1024],
                                            in_=src_sin[row * 128:(row + 1) * 128, :]),
                                wait=wtb, pool="dt", size=2)

                    def proj(wi, psum, extra=()):
                        def f(t, wi=wi, psum=psum, slot=slot, kt=kt):
                            for p_ in range(2):
                                t.matmul(psum[:, p_ * 512:(p_ + 1) * 512], ONE1[:],
                                         BIAS[0:1, wi * 1024 + p_ * 512:
                                              wi * 1024 + (p_ + 1) * 512],
                                         start=True, stop=False)
                            ins = None
                            for kc in range(8):
                                l = XTK[:, slot * 4096 + kc * 512 + kt * 128:
                                        slot * 4096 + kc * 512 + (kt + 1) * 128]
                                for p_ in range(2):
                                    r = WB[:, wi * 8192 + kc * 1024 + p_ * 512:
                                           wi * 8192 + kc * 1024 + (p_ + 1) * 512]
                                    ins = t.matmul(psum[:, p_ * 512:(p_ + 1) * 512],
                                                   l, r, start=False, stop=(kc == 7))
                            return ins
                        return E.op("tensor", f,
                                    wait=list(txt_list) + [tinit, tqsc, tbsc] + list(extra),
                                    inc=("pe", 1))

                    if which == "kv":
                        wk = [state["tVe"]] if state["tVe"] else []
                        tK = proj(1, PK, extra=wk)
                        tKr = rope_c(PK[:], tK, [tt1, tt2],
                                     [state["tT"]] if state["tT"] else [])
                        wv = [tKr]
                        tV = proj(2, PV_, extra=[tKr] +
                                  ([state["vsb_out"]] if kt == 0 and state["vsb_out"] else []))
                        tVe = E.op("vector", lambda v, kt=kt: v.tensor_copy(
                            VSB[:, kt * 1040:(kt + 1) * 1040]
                            .rearrange("p (h o) -> p h o", h=16)[:, :, 0:64],
                            PV_[:].rearrange("p (h o) -> p h o", h=16)),
                            wait=[tV, tvones])
                        state["tVe"] = tVe
                    else:
                        wk = [state["tKe"]] if state["tKe"] else []
                        tK = proj(0, PK, extra=wk)
                        tKr = rope_c(PK[:], tK, [tt1, tt2],
                                     [state["tT"]] if state["tT"] else [])
                    state["tbl"][gl] = E.tok("ve")

                    def ftr(t):
                        ins = None
                        for c in range(8):
                            ins = t.transpose(PSB[:, c * 128:(c + 1) * 128],
                                              RR[:, c * 128:(c + 1) * 128], IDB[:])
                        return ins
                    wtr = [tKr]
                    if state["psb"]:
                        wtr.append(state["psb"])
                    tT = E.op("tensor", ftr, wait=wtr, inc=("pe", 1))
                    state["tT"] = tT
                    wke = [state["ktb_out"]] if kt == 0 and state["ktb_out"] else []
                    tKe = E.op("vector", lambda v, kt=kt: v.tensor_copy(
                        KTB[:].rearrange("p (c k) -> p c k", c=8)[:, :, kt * 128:(kt + 1) * 128],
                        PSB[:].rearrange("p (c n) -> p c n", c=8)), wait=[tT] + wke)
                    state["psb"] = tKe
                    state["tKe"] = tKe
                state["slot_pe"][slot] = (tV if which == "kv" else tK)
                tko = E.dma("gpsimd", lambda g, blk=blk, out_dram=out_dram: g.dma_start(
                    out=out_dram.rearrange("(c p) k -> p c k", p=128)
                    [:, :, blk * 512:(blk + 1) * 512], in_=KTB[:]
                    .rearrange("p (c k) -> p c k", c=8)),
                    wait=[state["tKe"]], pool="dko", size=2)
                state["ktb_out"] = tko
                if which == "kv":
                    tvo = E.dma("gpsimd", lambda g, blk=blk: g.dma_start(
                        out=IO["V_d"].rearrange("(b p) n -> p b n", p=128)
                        [:, blk * 4:(blk + 1) * 4, :],
                        in_=VSB[:].rearrange("p (b n) -> p b n", b=4)),
                        wait=[state["tVe"]], pool="dko", size=2)
                    state["vsb_out"] = tvo

        if fused:
            build_pass("kv", 8, IO["SHX"], IO["cosK"], IO["sinK"], IO["KT_d"], 0)
            build_pass("q", 4, IO["X2bf_d"], IO["cosQ"], IO["sinQ"], IO["QT_d"], 0)
        else:
            build_pass("kv", 8, IO["x2T"], IO["cosK"], IO["sinK"], IO["KT_d"], 0)
            build_pass("q", 4, IO["x2Town"], IO["cosQ"], IO["sinQ"], IO["QT_d"], 0)
        tbuild = [state["ktb_out"], state["vsb_out"], state["tKe"], state["tVe"]]
        tbuild = [t for t in tbuild if t]

    with ExitStack() as st2:
        sb2 = lambda name, shape, dt: st2.enter_context(nc.sbuf_tensor(name, shape, dt))
        ps2 = lambda name, shape, dt=F32: st2.enter_context(nc.psum_tensor(name, shape, dt))
        KTF = sb2("c_KTF", [128, 8 * 4096], BF16)
        VF = sb2("c_VF", [128, 32 * 1040], BF16)
        QTB = sb2("c_QTB", [128, 8 * 512], BF16)
        WOH = sb2("c_WOH", [64, 16 * 1024], BF16)
        OB = sb2("c_OB", [1, 1024], F32)
        PTT = sb2("c_PTT", [128, 2048], BF16)
        ATT = sb2("c_AT", [64, 16 * 512], BF16)
        RB = sb2("c_RB", [128, 512], F32)
        PS_ = ps2("c_PS", [128, 2048])
        POa = ps2("c_POa", [128, 512])
        POb = ps2("c_POb", [128, 512])
        PB_ = ps2("c_PB", [64, 512])
        PY = PS_[:, 0:1024]

        twoh = E.dma("gpsimd", lambda g: g.dma_start(
            out=WOH[:], in_=IO["Wo_heads"]), pool="dk", size=2)
        tob = E.dma("gpsimd", lambda g: g.dma_start(
            out=OB[:], in_=IO["bqkvo"][0:1, 3 * 1024:4 * 1024]), pool="dk", size=2)
        tktf = E.dma("gpsimd", lambda g: g.dma_start(
            out=KTF[:].rearrange("p (c k) -> p c k", c=8),
            in_=IO["KT_d"].rearrange("(c p) k -> p c k", p=128)),
            wait=tbuild, pool="dk", size=2)
        tvf = E.dma("gpsimd", lambda g: g.dma_start(
            out=VF[:].rearrange("p (b n) -> p b n", b=32),
            in_=IO["V_d"].rearrange("(b p) n -> p b n", p=128)),
            wait=tbuild, pool="dk", size=2)

        state2 = {"qtb": {}, "exp": {}, "po_ev": {}, "rbc": None, "tres": None,
                  "res_done": {}, "oproj": None}
        for qb in range(4):
            slot = 0
            wq = [state2["qtb"][qb - 1]] if qb - 1 in state2["qtb"] else []
            wq = wq + ([state2["exp"]["last"]] if "last" in state2["exp"] else [])
            tqtb = E.dma("gpsimd", lambda g, qb=qb: g.dma_start(
                out=QTB[:].rearrange("p (c k) -> p c k", c=8),
                in_=IO["QT_d"].rearrange("(c p) k -> p c k", p=128)
                [:, :, qb * 512:(qb + 1) * 512]), wait=tbuild + wq, pool="dq", size=2)
            state2["qtb"][qb] = tqtb

            att_done = []
            for h in range(H):
                hb = 64 * (h & 1)
                fc = h >> 1
                PO = POa if h % 2 == 0 else POb
                wpo = [state2["po_ev"][h % 2]] if h % 2 in state2["po_ev"] else []
                pend = None
                for hk in range(16):
                    kb, half = divmod(hk, 2)
                    ss = hk % 2
                    def fsc(t, hb=hb, fc=fc, kb=kb, half=half, ss=ss):
                        ins = None
                        for k2 in range(2):
                            kt = half * 2 + k2
                            l = KTF[hb:hb + 64, fc * 4096 + kb * 512 + kt * 128:
                                    fc * 4096 + kb * 512 + (kt + 1) * 128]
                            r = QTB[hb:hb + 64, fc * 512:(fc + 1) * 512]
                            ins = t.matmul(PS_[:, ss * 1024 + k2 * 512:
                                               ss * 1024 + (k2 + 1) * 512], l, r,
                                           start=True, stop=True)
                        return ins
                    wsc = [tktf, tqtb]
                    prev = state2["exp"].get(ss)
                    if prev:
                        wsc.append(prev)
                    if (h, hk) == (0, 0) and state2["oproj"]:
                        wsc.append(state2["oproj"])
                    tsc = E.op("tensor", fsc, wait=wsc, inc=("pe", 1))
                    texp = E.op("scalar", lambda s, ss=ss: s.activation(
                        PTT[:, ss * 1024:(ss + 1) * 1024],
                        PS_[:, ss * 1024:(ss + 1) * 1024], AF.Exp), wait=[tsc])
                    state2["exp"][ss] = texp
                    state2["exp"]["last"] = texp
                    if pend is not None:
                        phk, ptexp = pend
                        pkb, phalf = divmod(phk, 2)
                        pss = phk % 2
                        def fpv(t, h=h, pkb=pkb, phalf=phalf, pss=pss, phk=phk, PO=PO):
                            ins = None
                            for k2 in range(2):
                                sl_ = pkb * 4 + phalf * 2 + k2
                                l = VF[:, sl_ * 1040 + h * 65: sl_ * 1040 + h * 65 + 65]
                                ins = t.matmul(PO[0:65, :], l,
                                               PTT[:, pss * 1024 + k2 * 512:
                                                   pss * 1024 + (k2 + 1) * 512],
                                               start=(phk == 0 and k2 == 0),
                                               stop=False,
                                               skip_group_check=True)
                            return ins
                        tpv = E.op("tensor", fpv, wait=[ptexp, tvf] + wpo, inc=("pe", 1))
                        wpo = []
                    pend = (hk, texp)
                phk, ptexp = pend
                pkb, phalf = divmod(phk, 2)
                pss = phk % 2
                def fpvL(t, h=h, pkb=pkb, phalf=phalf, pss=pss, PO=PO):
                    ins = None
                    for k2 in range(2):
                        sl_ = pkb * 4 + phalf * 2 + k2
                        l = VF[:, sl_ * 1040 + h * 65: sl_ * 1040 + h * 65 + 65]
                        ins = t.matmul(PO[0:65, :], l,
                                       PTT[:, pss * 1024 + k2 * 512:
                                           pss * 1024 + (k2 + 1) * 512],
                                       start=False, stop=(k2 == 1),
                                       skip_group_check=True)
                    return ins
                tpv = E.op("tensor", fpvL, wait=[ptexp, tvf] + wpo, inc=("pe", 1))
                tln = E.op("scalar", lambda s, PO=PO: s.activation(
                    RB[64:65, :], PO[64:65, :], AF.Ln), wait=[tpv])
                tex = E.op("scalar", lambda s: s.activation(
                    RB[64:65, :], RB[64:65, :], AF.Exp, scale=-1.0))
                wbc = [state2["rbc"]] if state2["rbc"] else []
                tbc2 = E.op("tensor", lambda t: t.matmul(
                    PB_[:], ONEC[64:65, :], RB[64:65, :], start=True, stop=True),
                    wait=[tex] + wbc, inc=("pe", 1))
                trbc = E.op("vector", lambda v: v.tensor_copy(RB[0:64, :], PB_[:]),
                            wait=[tbc2])
                state2["rbc"] = trbc
                tev = E.op("vector", lambda v, h=h, PO=PO: v.tensor_mul(
                    ATT[:, h * 512:(h + 1) * 512], PO[0:64, :], RB[0:64, :]),
                    wait=[trbc])
                state2["po_ev"][h % 2] = tev
                att_done.append(tev)

            for qt4 in range(4):
                qt = qb * 4 + qt4
                wxin = [state2["res_done"][qt - 1]] if qt - 1 in state2["res_done"] else []
                xin_tok = E.dma("sync", lambda s, qt=qt: s.dma_start(
                    out=XIN[:], in_=IO["x2own"][qt * 128:(qt + 1) * 128, :]),
                    wait=wxin + list(w0), pool="dx", size=2)

                def fo(t, qt4=qt4):
                    for p_ in range(2):
                        t.matmul(PY[:, p_ * 512:(p_ + 1) * 512], ONE1[:],
                                 OB[0:1, p_ * 512:(p_ + 1) * 512],
                                 start=True, stop=False)
                    ins = None
                    for h in range(H):
                        l = ATT[:, h * 512 + qt4 * 128: h * 512 + (qt4 + 1) * 128]
                        for p_ in range(2):
                            r = WOH[0:64, h * 1024 + p_ * 512: h * 1024 + (p_ + 1) * 512]
                            ins = t.matmul(PY[:, p_ * 512:(p_ + 1) * 512], l, r,
                                           start=False, stop=(h == H - 1))
                    return ins
                w_o = [att_done[-1], twoh, tob]
                if state2["oproj"]:
                    w_o.append(state2["oproj"])
                toproj = E.op("tensor", fo, wait=w_o, inc=("pe", 1))

                xr = XIN[:]
                tres = E.op("vector", lambda v, qt=qt, xr=xr: v.tensor_add(
                    xr, XIN[:], PY[:]), wait=[toproj, xin_tok])
                state2["oproj"] = tres
                tln2 = ln_plain(E, xr, xr, STATS[:], SCR[:], EPSC[:], wait=[tres])
                tst = E.dma("sync", lambda s, qt=qt, xr=xr: s.dma_start(
                    out=IO["out_x3"][qt * 128:(qt + 1) * 128, :], in_=xr),
                    wait=[tln2], pool="do", size=2)
                state2["res_done"][qt] = tst

    E.op("sync", lambda s: s.nop(), wait=E.pool_tokens("do", 2), inc=None)
    return st


def l1_rope_tables(positions):
    inv = (1.0 / 10000.0 ** (np.arange(0, D, 2, dtype=np.float32) / D)).astype(np.float32)
    ang = positions.astype(np.float32)[:, None] * inv[None, :]
    c = np.cos(ang).astype(np.float32)
    s = np.sin(ang).astype(np.float32)
    cos = np.stack([c, c], axis=-1).reshape(len(positions), D)
    sin = np.stack([-s, s], axis=-1).reshape(len(positions), D)
    return np.ascontiguousarray(cos), np.ascontiguousarray(sin)


def prep_stage_c(I, x2):
    bias = np.ascontiguousarray(np.stack([I["L1_attn_bq"], I["L1_attn_bk"],
                                          I["L1_attn_bv"], I["L1_attn_b0"]])
                                .reshape(1, 4 * D).astype(np.float32))
    ident = np.eye(128, dtype=np.float32)
    cosK, sinK = l1_rope_tables(np.arange(4096))
    common = {
        "Wq": np.ascontiguousarray(I["L1_attn_Wq"].astype(np.float32)),
        "Wk": np.ascontiguousarray(I["L1_attn_Wk"].astype(np.float32)),
        "Wv": np.ascontiguousarray(I["L1_attn_Wv"].astype(np.float32)),
        "Wo": np.ascontiguousarray(I["L1_attn_W0"].astype(np.float32)),
        "Wo_heads": np.ascontiguousarray(
            I["L1_attn_W0"].astype(np.float32).reshape(16, 64, 1024)
            .transpose(1, 0, 2).reshape(64, 16 * 1024)),
        "bqkvo": bias, "ident": ident, "cosK": cosK, "sinK": sinK,
    }
    maps = []
    for c in range(NC_):
        b, h = c // 2, c % 2
        m = dict(common)
        xb = np.ascontiguousarray(x2[b])
        m["x2T"] = np.ascontiguousarray(xb.T)
        m["x2Town"] = np.ascontiguousarray(xb[h * T:(h + 1) * T].T)
        m["x2own"] = np.ascontiguousarray(xb[h * T:(h + 1) * T])
        cq, sq = l1_rope_tables(np.arange(h * T, (h + 1) * T))
        m["cosQ"] = cq
        m["sinQ"] = sq
        maps.append(m)
    return maps


def build_stage_c(nc):
    IO = {}
    IO["x2T"] = nc.dram_tensor("x2T", [D, 4096], F32, kind="ExternalInput").ap()
    IO["x2Town"] = nc.dram_tensor("x2Town", [D, T], F32, kind="ExternalInput").ap()
    IO["x2own"] = nc.dram_tensor("x2own", [T, D], F32, kind="ExternalInput").ap()
    for w in ("Wq", "Wk", "Wv", "Wo"):
        IO[w] = nc.dram_tensor(w, [D, D], F32, kind="ExternalInput").ap()
    IO["bqkvo"] = nc.dram_tensor("bqkvo", [1, 4 * D], F32, kind="ExternalInput").ap()
    IO["cosK"] = nc.dram_tensor("cosK", [4096, D], F32, kind="ExternalInput").ap()
    IO["sinK"] = nc.dram_tensor("sinK", [4096, D], F32, kind="ExternalInput").ap()
    IO["cosQ"] = nc.dram_tensor("cosQ", [T, D], F32, kind="ExternalInput").ap()
    IO["sinQ"] = nc.dram_tensor("sinQ", [T, D], F32, kind="ExternalInput").ap()
    IO["ident"] = nc.dram_tensor("ident", [128, 128], F32, kind="ExternalInput").ap()
    IO["c_ident"] = IO["ident"]
    IO["Wo_heads"] = nc.dram_tensor("Wo_heads", [64, 16 * 1024], F32, kind="ExternalInput").ap()
    IO["KT_d"] = nc.dram_tensor("KT_d", [D, 4096], BF16, kind="Internal").ap()
    IO["QT_d"] = nc.dram_tensor("QT_d", [D, T], BF16, kind="Internal").ap()
    IO["V_d"] = nc.dram_tensor("V_d", [4096, 1040], BF16, kind="Internal").ap()
    IO["out_x3"] = nc.dram_tensor("out_x3", [T, D], F32, kind="ExternalOutput").ap()
    E = Em()
    stc = emit_l1_attn(nc, E, IO)
    with ExitStack() as stack:
        E.play(nc, stack)
    stc.close()
    return nc


def run_stage_c(I, x2):
    nc = bass.Bass("TRN2", target_bir_lowering=False)
    build_stage_c(nc)
    maps = prep_stage_c(I, x2)
    res = run_bass_kernel_spmd(nc, maps, core_ids=list(range(NC_)))
    x3 = np.empty((4, 4096, 1024), np.float32)
    for c in range(NC_):
        b, h = c // 2, c % 2
        x3[b, h * T:(h + 1) * T] = res.results[c]["out_x3"]
    return x3



def kern_D_PARTS():
    return 8

def emit_l1_ffn(nc, E, IO, fused=False, w0=()):
    st = ExitStack()
    sb = lambda name, shape, dt: st.enter_context(nc.sbuf_tensor(name, shape, dt))
    ps = lambda name, shape, dt=F32: st.enter_context(nc.psum_tensor(name, shape, dt))

    X = sb("d_X", [128, 16 * 1024], F32)
    NTB = sb("d_NTB", [128, 8 * 2048], BF16)
    NTF = sb("d_NTF", [128, 8 * 128], F32)
    W1Q = sb("d_W1Q", [128, 2 * 8 * 1024], BF16)
    W2Q = sb("d_W2Q", [128, 2 * 8 * 1024], BF16)
    GWF = sb("d_GWF", [128, 8 * 8], F32)
    GB1 = sb("d_GB1", [1, 8], F32)
    GB2 = sb("d_GB2", [1, 8], F32)
    B1C = sb("d_B1C", [128, 64], F32)
    B2 = sb("d_B2", [1, 2 * 1024], F32)
    ONE1 = sb("d_ONE1", [1, 128], F32)
    EPSC = sb("d_EPSC", [128, 1], F32)
    STATS = sb("d_STATS", [128, 8], F32)
    SCR = sb("d_SCR", [128, 1024], F32)
    LG = sb("d_LG", [128, 16 * 8], F32)
    MX = sb("d_MX", [128, 16], F32)
    MSK = sb("d_MSK", [128, 16 * 8], F32)
    HMA = sb("d_HMA", [128, 8 * 512], BF16)
    XO = sb("d_XO", [128, 1024], F32)

    PYA = ps("d_PYA", [128, 1024])
    PYB = ps("d_PYB", [128, 1024])
    PH = ps("d_PH", [128, 512])
    PH2 = ps("d_PH2", [128, 512])
    PSM = ps("d_PSM", [128, 256])

    IDF = sb("d_IDF", [128, 128], F32) if fused else None
    ltoks = []
    ltoks.append(E.dma("gpsimd", lambda g: g.dma_start(
        out=X[:].rearrange("p (t d) -> p t d", t=16),
        in_=IO["x3"].rearrange("(t p) d -> p t d", p=128)), wait=list(w0), pool="di"))
    if fused:
        ltoks.append(E.dma("gpsimd", lambda g: g.dma_start(out=IDF[:], in_=IO["d_ident"]),
                           pool="di"))
    else:
        ltoks.append(E.dma("gpsimd", lambda g: g.dma_start(
            out=NTB[:].rearrange("p (c n) -> p c n", c=8),
            in_=IO["x3T"].rearrange("(c p) n -> p c n", p=128)), pool="di"))
    ltoks.append(E.dma("gpsimd", lambda g: g.dma_start(
        out=GWF[:].rearrange("p (c n) -> p c n", c=8),
        in_=IO["gate_wT"].rearrange("(c p) n -> p c n", p=128)), pool="di"))
    ltoks.append(E.dma("gpsimd", lambda g: g.dma_start(out=GB1[:], in_=IO["gb2"][0:1, :]), pool="di"))
    ltoks.append(E.dma("gpsimd", lambda g: g.dma_start(out=GB2[:], in_=IO["gb2"][1:2, :]), pool="di"))
    tload = ltoks
    tinit = E.op("vector", lambda v: v.memset(EPSC[:], EPS))
    t1on = E.op("vector", lambda v: v.memset(ONE1[:], 1.0))

    prev_lg = None
    prev_tp = None
    for t_ in range(16):
        tef2 = None
        if fused:
            def ftp(t, t_=t_):
                ins = None
                for c in range(8):
                    ins = t.transpose(PYA[:, c * 128:(c + 1) * 128],
                                      X[:, t_ * 1024 + c * 128:t_ * 1024 + (c + 1) * 128],
                                      IDF[:])
                return ins
            wtp = list(tload) + ([prev_lg] if prev_lg else [])
            ttp = E.op("tensor", ftp, wait=wtp, inc=("pe", 1))
            prev_tp = ttp
            tef2 = E.op("vector", lambda v: v.tensor_copy(NTF[:], PYA[:]), wait=[ttp])
            teb2 = E.op("scalar", lambda s, t_=t_: s.activation(
                NTB[:].rearrange("p (c n) -> p c n", c=8)[:, :, t_ * 128:(t_ + 1) * 128],
                PYA[:].rearrange("p (c n) -> p c n", c=8), AF.Copy), wait=[ttp, tef2])
            tntf = teb2
        else:
            tntf = E.dma("sync", lambda s, t_=t_: s.dma_start(
                out=NTF[:].rearrange("p (c n) -> p c n", c=8),
                in_=IO["x3T"].rearrange("(c p) n -> p c n", p=128)
                [:, :, t_ * 128:(t_ + 1) * 128]), wait=[prev_lg], pool="dx", size=2)

        def frt(t, t_=t_):
            t.matmul(PSM[:, 0:8], ONE1[:], GB1[:], start=True, stop=False)
            t.matmul(PSM[:, 0:8], ONE1[:], GB2[:], start=False, stop=False)
            ins = None
            for kc in range(8):
                ins = t.matmul(PSM[:, 0:8], NTF[:, kc * 128:(kc + 1) * 128],
                               GWF[:, kc * 8:(kc + 1) * 8],
                               start=False, stop=(kc == 7))
            return ins
        trt = E.op("tensor", frt, wait=[tntf, t1on, tef2] + list(tload),
                   inc=("pe", 1))
        tlg = E.op("vector", lambda v, t_=t_: v.tensor_copy(
            LG[:, t_ * 8:(t_ + 1) * 8], PSM[:, 0:8]), wait=[trt])
        prev_lg = tlg
        tmx = E.op("vector", lambda v, t_=t_: v.reduce_max(
            MX[:, t_:t_ + 1], LG[:, t_ * 8:(t_ + 1) * 8], axis=AX))
        tms = E.op("vector", lambda v, t_=t_: v.tensor_scalar(
            MSK[:, t_ * 8:(t_ + 1) * 8], LG[:, t_ * 8:(t_ + 1) * 8],
            MX[:, t_:t_ + 1], None, ALU.is_ge))
    tmask_done = E.tok("ve")

    if kern_D_PARTS() < 1:
        for t_ in range(16):
            E.dma("sync", lambda s, t_=t_: s.dma_start(
                out=IO["out_x4"][t_ * 128:(t_ + 1) * 128, :],
                in_=X[:, t_ * 1024:(t_ + 1) * 1024]),
                wait=[E.tok("ve")], pool="do", size=2)
        E.op("sync", lambda s: s.nop(), wait=E.pool_tokens("do", 2), inc=None)
        return st
    state = {"xadd": {}, "t2": None, "hm_used": None}
    steps = [(e, fq) for e in range(kern_D_PARTS() if kern_D_PARTS() <= 8 else 8)
             for fq in range(4)]
    for i, (e, fq) in enumerate(steps):
        sl = i % 2
        wdep = [state["xadd"].get(i - 2)]
        tw1 = E.dma("gpsimd", lambda g, e=e, fq=fq, sl=sl: g.dma_start(
            out=W1Q[:, sl * 8192:(sl + 1) * 8192].rearrange("p (c n) -> p c n", c=8),
            in_=IO["w1"][e].rearrange("(c p) n -> p c n", p=128)
            [:, :, fq * 1024:(fq + 1) * 1024]), wait=wdep, pool="dw", size=6)
        tw2 = E.dma("gpsimd", lambda g, e=e, fq=fq, sl=sl: g.dma_start(
            out=W2Q[:, sl * 8192:(sl + 1) * 8192].rearrange("p (c n) -> p c n", c=8),
            in_=IO["w2"][e].rearrange("(c p) n -> p c n", p=128)
            [:, fq * 8:(fq + 1) * 8, :]), wait=wdep, pool="dw", size=6)
        tb1 = E.dma("gpsimd", lambda g, e=e, sl=sl: g.dma_start(
            out=B1C[:, sl * 32:sl * 32 + 32], in_=IO["b1c"][e]),
            wait=wdep, pool="dw", size=6)
        tb2 = E.dma("gpsimd", lambda g, e=e, sl=sl: g.dma_start(
            out=B2[:, sl * 1024:(sl + 1) * 1024], in_=IO["b2"][e:e + 1, :]),
            wait=wdep, pool="dw", size=6)
        twl = [tw1, tw2, tb1, tb2]
        last_add = state["xadd"].get(i - 1) or state["xadd"].get(i - 2)
        for blk in range(4):
            xadd_prev = last_add
            hm_prev = state["hm_used"]
            relu_prev = None
            relu_prev2 = None
            for fc in range(8):
                PHc = PH if fc % 2 == 0 else PH2
                def fh1(t, blk=blk, fc=fc, sl=sl, PHc=PHc):
                    ins = None
                    for kc in range(8):
                        ins = t.matmul(PHc[:], W1Q[:, sl * 8192 + kc * 1024 + fc * 128:
                                                  sl * 8192 + kc * 1024 + (fc + 1) * 128],
                                       NTB[:, kc * 2048 + blk * 512:
                                           kc * 2048 + (blk + 1) * 512],
                                       start=(kc == 0), stop=(kc == 7))
                    return ins
                wg = list(twl) + ([relu_prev2] if relu_prev2 else [])
                if i == 0 and blk == 0 and fc == 0:
                    wg.append(tntf)
                    wg.append(tmask_done)
                if fc == 0 and hm_prev:
                    wg.append(hm_prev)
                th1 = E.op("tensor", fh1, wait=wg, inc=("pe", 1))
                trl = E.op("scalar", lambda s, fq=fq, fc=fc, sl=sl, blk=blk, PHc=PHc:
                           s.activation(HMA[:, fc * 512:(fc + 1) * 512], PHc[:], AF.Relu,
                                        bias=B1C[:, sl * 32 + fq * 8 + fc:
                                                 sl * 32 + fq * 8 + fc + 1]),
                           wait=[th1])
                relu_prev2 = relu_prev
                relu_prev = trl
            tdm = None
            tadd2 = None
            for h2 in range(2):
                def fy(t, h2=h2, sl=sl, e=e, fq=fq):
                    ins = None
                    if fq == 3:
                        for sub in range(2):
                            for p_ in range(2):
                                t.matmul(
                                    (PYA if sub == 0 else PYB)[:, p_ * 512:(p_ + 1) * 512],
                                    ONE1[:],
                                    B2[0:1, sl * 1024 + p_ * 512: sl * 1024 + (p_ + 1) * 512],
                                    start=True, stop=False)
                    for fc in range(8):
                        for sub in range(2):
                            l = HMA[:, fc * 512 + h2 * 256 + sub * 128:
                                    fc * 512 + h2 * 256 + (sub + 1) * 128]
                            for p_ in range(2):
                                ins = t.matmul(
                                    (PYA if sub == 0 else PYB)[:, p_ * 512:(p_ + 1) * 512],
                                    l,
                                    W2Q[:, sl * 8192 + fc * 1024 + p_ * 512:
                                        sl * 8192 + fc * 1024 + (p_ + 1) * 512],
                                    start=(fc == 0 and fq != 3), stop=(fc == 7))
                    return ins
                wd = [relu_prev]
                if xadd_prev:
                    wd.append(xadd_prev)
                if tadd2:
                    wd.append(tadd2)
                tdm = E.op("tensor", fy, wait=wd, inc=("pe", 1))
                base = (4 * blk + 2 * h2) * 1024
                mcol = (4 * blk + 2 * h2) * 8 + e
                ta1 = E.op("vector", lambda v, base=base, mcol=mcol: v.scalar_tensor_tensor(
                    X[:, base:base + 1024], PYA[:],
                    MSK[:, mcol:mcol + 1], X[:, base:base + 1024],
                    ALU.mult, ALU.add), wait=[tdm, tmask_done] + list(tload))
                ta2 = E.op("vector", lambda v, base=base, mcol=mcol + 8: v.scalar_tensor_tensor(
                    X[:, base + 1024:base + 2048], PYB[:],
                    MSK[:, mcol:mcol + 1], X[:, base + 1024:base + 2048],
                    ALU.mult, ALU.add))
                tadd2 = ta2
            state["hm_used"] = tdm
            last_add = ta2
            state["xadd"][i] = ta2
    state["xadd"] = state["xadd"][len(steps) - 1]

    otoks = {}
    for t_ in range(16):
        xt = X[:, t_ * 1024:(t_ + 1) * 1024]
        xo = XO[:]
        wln = [state["xadd"]]
        if t_ - 1 in otoks:
            wln.append(otoks[t_ - 1])
        if (t_ - 1, "b") in otoks:
            wln.append(otoks[(t_ - 1, "b")])
        tln = ln_plain(E, xt, xo, STATS[:], SCR[:], EPSC[:], wait=wln)
        otoks[t_] = E.dma("sync", lambda s, t_=t_, xo=xo: s.dma_start(
            out=IO["out_x4"][t_ * 128:(t_ + 1) * 128, :], in_=xo),
            wait=[tln], pool="do", size=2)

    E.op("sync", lambda s: s.nop(), wait=E.pool_tokens("do", 2), inc=None)
    return st


def prep_stage_d(I, x3):
    ident = np.eye(128, dtype=np.float32)
    gb2 = np.stack([I["L1_ffn_gate_b"], I["L1_ffn_expert_bias"]]).astype(np.float32)
    b1c = np.ascontiguousarray(I["L1_ffn_b1"].astype(np.float32)
                               .reshape(8, 32, 128).transpose(0, 2, 1))
    common = {
        "gate_wT": np.ascontiguousarray(I["L1_ffn_gate_w"].astype(np.float32).T),
        "gb2": np.ascontiguousarray(gb2),
        "w1": np.ascontiguousarray(I["L1_ffn_w1"].astype(np.float32)),
        "b1c": b1c,
        "w2": np.ascontiguousarray(I["L1_ffn_w2"].astype(np.float32)),
        "b2": np.ascontiguousarray(I["L1_ffn_b2"].astype(np.float32)),
        "ident": ident,
    }
    maps = []
    for c in range(NC_):
        b, h = c // 2, c % 2
        m = dict(common)
        xb = np.ascontiguousarray(x3[b, h * T:(h + 1) * T])
        m["x3"] = xb
        m["x3T"] = np.ascontiguousarray(xb.T)
        maps.append(m)
    return maps


def build_stage_d(nc):
    IO = {}
    IO["x3"] = nc.dram_tensor("x3", [T, D], F32, kind="ExternalInput").ap()
    IO["x3T"] = nc.dram_tensor("x3T", [D, T], F32, kind="ExternalInput").ap()
    IO["gate_wT"] = nc.dram_tensor("gate_wT", [D, 8], F32, kind="ExternalInput").ap()
    IO["gb2"] = nc.dram_tensor("gb2", [2, 8], F32, kind="ExternalInput").ap()
    IO["w1"] = nc.dram_tensor("w1", [8, D, 4096], F32, kind="ExternalInput").ap()
    IO["b1c"] = nc.dram_tensor("b1c", [8, 128, 32], F32, kind="ExternalInput").ap()
    IO["w2"] = nc.dram_tensor("w2", [8, 4096, D], F32, kind="ExternalInput").ap()
    IO["b2"] = nc.dram_tensor("b2", [8, D], F32, kind="ExternalInput").ap()
    IO["ident"] = nc.dram_tensor("ident", [128, 128], F32, kind="ExternalInput").ap()
    IO["out_x4"] = nc.dram_tensor("out_x4", [T, D], F32, kind="ExternalOutput").ap()
    E = Em()
    stc = emit_l1_ffn(nc, E, IO)
    with ExitStack() as stack:
        E.play(nc, stack)
    stc.close()
    return nc


def run_stage_d(I, x3):
    nc = bass.Bass("TRN2", target_bir_lowering=False)
    build_stage_d(nc)
    maps = prep_stage_d(I, x3)
    res = run_bass_kernel_spmd(nc, maps, core_ids=list(range(NC_)))
    x4 = np.empty((4, 4096, 1024), np.float32)
    for c in range(NC_):
        b, h = c // 2, c % 2
        x4[b, h * T:(h + 1) * T] = res.results[c]["out_x4"]
    return x4



def build_fused(nc):
    dt_ = lambda n, s_, d, k="ExternalInput", a="Local": nc.dram_tensor(
        n, s_, d, kind=k, addr_space=a).ap()
    ident = dt_("ident", [128, 128], F32)
    nonce = dt_("nonce", [1, 8], I32)
    A = {}
    A["x_tm"] = dt_("x_tm", [TB, D], F32)
    A["x_tr"] = dt_("x_tr", [D, TB], F32)
    for w in ("WqT", "WkT", "WvT", "WoT"):
        A[w] = dt_(w, [D, D], F32)
    A["bqkvo"] = dt_("bqkvo", [1, 4 * D], F32)
    A["cosL0"] = dt_("cosL0", [TB, D], F32)
    A["sinL0"] = dt_("sinL0", [TB, D], F32)
    A["maskN"] = dt_("maskN", [128, 256], BF16)
    A["maskF"] = dt_("maskF", [128, 256], BF16)
    A["ident"] = ident
    A["out_x1"] = dt_("X1_d", [T, D], F32, k="Internal")
    B = {}
    B["x1"] = A["out_x1"]
    B["lnwb"] = dt_("lnwb", [1, 2048], F32)
    B["router_w"] = dt_("router_w", [D, 8], F32)
    B["gate_w"] = dt_("gate_w", [8, D, 2048], F32)
    B["up_w"] = dt_("up_w", [8, D, 2048], F32)
    B["down_w"] = dt_("down_w", [8, 2048, D], F32)
    B["sh_gate"] = dt_("sh_gate", [D, 2048], F32)
    B["sh_up"] = dt_("sh_up", [D, 2048], F32)
    B["sh_down"] = dt_("sh_down", [2048, D], F32)
    B["ident"] = ident
    B["w8scr"] = dt_("w8scr", [8, 2048], F32, k="Internal")
    B["out_x2"] = dt_("X2_d", [T, D], F32, k="Internal")
    B["X2bf_d"] = dt_("X2bf_d", [T, D], BF16, k="Internal")
    B["SHX"] = dt_("SHX", [4096, D], BF16, k="Internal", a="Shared")
    B["FLAG"] = dt_("FLAG", [2, 8], I32, k="Internal", a="Shared")
    B["nonce"] = nonce
    C = {}
    C["c_ident"] = ident
    for w in ("Wq", "Wk", "Wv", "Wo"):
        C[w] = dt_("c_" + w, [D, D], F32)
    C["bqkvo"] = dt_("c_bqkvo", [1, 4 * D], F32)
    C["cosK"] = dt_("cosK", [4096, D], F32)
    C["sinK"] = dt_("sinK", [4096, D], F32)
    C["cosQ"] = dt_("cosQ", [T, D], F32)
    C["sinQ"] = dt_("sinQ", [T, D], F32)
    C["Wo_heads"] = dt_("Wo_heads", [64, 16 * 1024], F32)
    C["KT_d"] = dt_("KT_d", [D, 4096], BF16, k="Internal")
    C["QT_d"] = dt_("QT_d", [D, T], BF16, k="Internal")
    C["V_d"] = dt_("V_d", [4096, 1040], BF16, k="Internal")
    C["SHX"] = B["SHX"]
    C["X2bf_d"] = B["X2bf_d"]
    C["FLAG"] = B["FLAG"]
    C["nonce"] = nonce
    C["x2own"] = B["out_x2"]
    C["out_x3"] = dt_("X3_d", [T, D], F32, k="Internal")
    Dd = {}
    Dd["x3"] = C["out_x3"]
    Dd["d_ident"] = ident
    Dd["gate_wT"] = dt_("gate_wT", [D, 8], F32)
    Dd["gb2"] = dt_("gb2", [2, 8], F32)
    Dd["w1"] = dt_("w1", [8, D, 4096], F32)
    Dd["b1c"] = dt_("b1c", [8, 128, 32], F32)
    Dd["w2"] = dt_("w2", [8, 4096, D], F32)
    Dd["b2"] = dt_("b2", [8, D], F32)
    Dd["out_x4"] = dt_("out_x4", [T, D], F32, k="ExternalOutput")

    E = Em()
    s1 = emit_l0_attn(nc, E, A)
    E.stage_barrier(); s1.close()
    w0b = E.pool_tokens("do", 2)
    s2 = emit_l0_ffn(nc, E, B, fused=True, w0=w0b)
    E.stage_barrier(); s2.close()
    w0c = E.pool_tokens("do", 2)
    s3 = emit_l1_attn(nc, E, C, fused=True, w0=w0c)
    E.stage_barrier(); s3.close()
    w0d = E.pool_tokens("do", 2)
    s4 = emit_l1_ffn(nc, E, Dd, fused=True, w0=w0d)
    with ExitStack() as stack:
        E.play(nc, stack)
    s4.close()
    return nc


_NONCE_CTR = [0]


def prep_fused(I):
    a_maps = prep_stage_a(I)
    b_common = prep_stage_b(I, np.zeros((4, 4096, 1024), np.float32))[0]
    c_common = prep_stage_c(I, np.zeros((4, 4096, 1024), np.float32))
    d_common = prep_stage_d(I, np.zeros((4, 4096, 1024), np.float32))[0]
    _NONCE_CTR[0] += 1
    nonce = np.full((1, 8), int((np.random.randint(1, 1 << 20) * 1024 +
                                 _NONCE_CTR[0]) | 1) & 0x7FFFFFFF, dtype=np.int32)
    maps = []
    for c in range(NC_):
        m = {}
        m.update(a_maps[c])
        for k in ("lnwb", "router_w", "gate_w", "up_w", "down_w",
                  "sh_gate", "sh_up", "sh_down"):
            m[k] = b_common[k]
        cm = c_common[c]
        for k, name in (("Wq", "c_Wq"), ("Wk", "c_Wk"), ("Wv", "c_Wv"),
                        ("Wo", "c_Wo"), ("bqkvo", "c_bqkvo"), ("cosK", "cosK"),
                        ("sinK", "sinK"), ("cosQ", "cosQ"), ("sinQ", "sinQ"),
                        ("Wo_heads", "Wo_heads")):
            m[name] = cm[k]
        for k in ("gate_wT", "gb2", "w1", "b1c", "w2", "b2"):
            m[k] = d_common[k]
        m["nonce"] = nonce
        maps.append(m)
    return maps



_FUSED_NC = []


def kernel(**inputs):
    if not _FUSED_NC:
        nc = bass.Bass("TRN2", target_bir_lowering=False)
        build_fused(nc)
        _FUSED_NC.append(nc)
    nc = _FUSED_NC[0]
    maps = prep_fused(inputs)
    res = run_bass_kernel_spmd(nc, maps, core_ids=list(range(NC_)))
    x4 = np.empty((4, 4096, 1024), np.float32)
    for c in range(NC_):
        b, h = c // 2, c % 2
        x4[b, h * T:(h + 1) * T] = res.results[c]["out_x4"]
    return x4
